# version 05 (final)
# baseline (speedup 1.0000x reference)
import math
from contextlib import ExitStack

import numpy as np
import ml_dtypes

import concourse.bass as bass
import concourse.mybir as mybir
from concourse.bass_utils import run_bass_kernel_spmd

F32 = mybir.dt.float32
BF16 = mybir.dt.bfloat16
I32 = mybir.dt.int32
AF = mybir.ActivationFunctionType
ALU = mybir.AluOpType
BF = ml_dtypes.bfloat16

NCORES = 8
B = 2
S = 4096
D = 4096
TOK = B * S
TPC = TOK // NCORES
KREM = 410
NEG = -1.0e30
BIGP = 128.0
BIS_W0 = 48.0
N_ROUNDS = 16
LN_EPS = 1e-5
FFN_EPS = 1e-12
ER = 128

_SKIP_COLLECTIVES = False
_SKIP_SWITCH = False
_NPHASE = 4
_P2_BH = 8
_P2_STAGE = 6
_P2_QC = 8
_P2_QC0 = 0
_P2_SERIAL = False
_DBG_OUT = False


def _host_consts():
    inv_freq = 1.0 / (10000.0 ** (np.arange(0, 128, 2, dtype=np.float64) / 128))
    t = np.arange(S, dtype=np.float64)
    freqs = np.outer(t, inv_freq)
    emb = np.concatenate([freqs, freqs], axis=1)
    cos = np.cos(emb).T.astype(np.float32)
    sinf = np.sin(emb).T.astype(np.float32)
    sinf[:64, :] *= -1.0
    sc = np.float32(1.0 / math.sqrt(128.0))
    tri = np.zeros((128, 128), dtype=np.float32)
    tri[np.triu_indices(128, 1)] = NEG
    trip = np.zeros((128, 4, 512), dtype=np.float32)
    for i in range(4):
        n = (i + 1) * 128
        trip[:, i, n - 128:n] = tri
    trip = trip.reshape(128, 2048)
    wh = np.zeros((128, 32), dtype=np.float32)
    for jq in range(32):
        wh[:, jq] = (jq + 1) * 128 - 2 * KREM
    kam = np.zeros((128, 32), dtype=np.float32)
    for jq in range(32):
        for rr in range(128):
            kam[rr, jq] = 1.0 if jq * 128 + rr + 1 > KREM else 0.0
    e8 = np.zeros((16, 128), dtype=np.float32)
    for ee in range(16):
        e8[ee, ee * 8:(ee + 1) * 8] = 1.0
    return {
        "ropeC": cos, "ropeS": sinf,
        "ropeCq": (cos * sc).astype(np.float32),
        "ropeSq": (sinf * sc).astype(np.float32),
        "TRI": tri.astype(BF),
        "TRIP": trip.astype(BF),
        "IDB": np.eye(128, dtype=BF),
        "IDF": np.eye(128, dtype=np.float32),
        "ONESF": np.ones((128, 1), dtype=np.float32),
        "IOTA8": np.broadcast_to(np.arange(8, dtype=np.float32), (128, 8)).copy(),
        "IOTA4r": np.broadcast_to(np.arange(16, dtype=np.float32), (4, 16)).copy(),
        "E8F": e8,
        "ONer": np.ones((1, 128), dtype=np.float32),
        "WH820": wh,
        "KAM": kam,
    }


_CONST_SPECS = {
    "ropeC": ([128, S], F32), "ropeS": ([128, S], F32),
    "ropeCq": ([128, S], F32), "ropeSq": ([128, S], F32),
    "TRI": ([128, 128], BF16), "TRIP": ([128, 2048], BF16), "IDB": ([128, 128], BF16), "IDF": ([128, 128], F32),
    "ONESF": ([128, 1], F32), "IOTA8": ([128, 8], F32), "WH820": ([128, 32], F32),
    "KAM": ([128, 32], F32),
    "IOTA4r": ([4, 16], F32), "E8F": ([16, 128], F32), "ONer": ([1, 128], F32),
}
_STREAMED = {"ropeC", "ropeS", "ropeCq", "ropeSq"}


class SemT:
    def __init__(self, h):
        self.h = h
        self.v = 0

    def inc(self, instr, k=1):
        instr.then_inc(self.h, k)
        self.v += k
        return self.v

    def wait(self, eng, val=None):
        eng.wait_ge(self.h, self.v if val is None else val)


def build_program():
    nc = bass.Bass("TRN2", target_bir_lowering=False)

    x_slice = nc.dram_tensor("x_slice", [TPC, D], F32, kind="ExternalInput")
    wqT = nc.dram_tensor("wqT", [D, 512], F32, kind="ExternalInput")
    wkT = nc.dram_tensor("wkT", [D, 128], F32, kind="ExternalInput")
    wvT = nc.dram_tensor("wvT", [D, 128], F32, kind="ExternalInput")
    woT = nc.dram_tensor("woT", [32, D, 128], F32, kind="ExternalInput")
    wbT = nc.dram_tensor("wbT", [32, D, 128], F32, kind="ExternalInput")
    aallT = nc.dram_tensor("aallT", [D, ER], F32, kind="ExternalInput")
    ball_d = nc.dram_tensor("ball_d", [ER, D], F32, kind="ExternalInput")
    routerT = nc.dram_tensor("routerT", [D, 16], F32, kind="ExternalInput")
    router_b = nc.dram_tensor("router_b", [16, 1], F32, kind="ExternalInput")
    prior_in = nc.dram_tensor("prior_in", [16, 1], F32, kind="ExternalInput")
    active_in = nc.dram_tensor("active_in", [4, 1], I32, kind="ExternalInput")
    b_base_t = nc.dram_tensor("b_base_t", [128, 32], F32, kind="ExternalInput")
    cdram = {k: nc.dram_tensor(k, sh, dt, kind="ExternalInput")
             for k, (sh, dt) in _CONST_SPECS.items()}
    y_out = nc.dram_tensor("y_out", [TPC, D], F32, kind="ExternalOutput")

    xTc_in = nc.dram_tensor("xTc_in", [D, TPC], BF16)
    xg = nc.dram_tensor("xg", [NCORES * D, TPC], BF16)
    xTf32 = nc.dram_tensor("xTf32", [D, TPC], F32)
    qTd = nc.dram_tensor("qTd", [512, TOK], BF16)
    oc_in = nc.dram_tensor("oc_in", [512, TOK], BF16)
    og = nc.dram_tensor("og", [NCORES * 512, TOK], BF16)
    dbgkind = {"kind": "ExternalOutput"} if _DBG_OUT else {}
    r1d = nc.dram_tensor("r1d", [D, TPC], F32, **dbgkind)
    r2d = nc.dram_tensor("r2d", [D, TPC], F32, **dbgkind)

    with ExitStack() as top:
        e = top.enter_context
        cs = {k: e(nc.sbuf_tensor("c_" + k, sh, dt))
              for k, (sh, dt) in _CONST_SPECS.items() if k not in _STREAMED}
        bbt = e(nc.sbuf_tensor("bbt", [128, 32], F32))
        KT = e(nc.sbuf_tensor("KT", [128, TOK], BF16))
        VT = e(nc.sbuf_tensor("VT", [128, 64 * 128], BF16))
        prior_sb = e(nc.sbuf_tensor("prior_sb", [16, 1], F32))
        rb_sb = e(nc.sbuf_tensor("rb_sb", [16, 1], F32))
        act_i = e(nc.sbuf_tensor("act_i", [4, 1], I32))
        act_f = e(nc.sbuf_tensor("act_f", [4, 1], F32))
        sel4 = e(nc.sbuf_tensor("sel4", [4, 16], F32))
        selT = e(nc.sbuf_tensor("selT", [16, 4], F32))

        names = ["sc", "sw", "sx0", "st0", "sv0", "sd0", "sg1", "ssel",
                 "sxg", "srp", "smm", "sev", "svt", "sve", "sqd",
                 "sqh", "ss1", "sse", "smd", "scn", "sfin", "spm", "spn",
                 "spt", "spe", "sav", "soe", "sos", "sg2",
                 "sogl", "sfw", "sxt", "smw", "sr1e", "sq2", "sst", "srl",
                 "sx2", "sal", "sr2e", "syt", "sye", "syd",
                 "sd0b", "srpb", "sxgb", "sqd1", "sqd2", "sqd3", "srll",
                 "sx0b", "sxtb", "srllb", "sw2a", "sw2b", "sb"]
        SM = {n: SemT(e(nc.semaphore(n))) for n in names}
        blk = e(nc.Block())
        SC, SW = SM["sc"], SM["sw"]

        @blk.sync
        def _(q):
            for k in cs:
                SC.inc(q.dma_start(out=cs[k][:], in_=cdram[k][:]), 16)
            SC.inc(q.dma_start(out=bbt[:], in_=b_base_t[:]), 16)
            SC.inc(q.dma_start(out=prior_sb[:], in_=prior_in[:]), 16)
            SC.inc(q.dma_start(out=rb_sb[:], in_=router_b[:]), 16)
            SC.inc(q.dma_start(out=act_i[:], in_=active_in[:]), 16)

        SX0, ST0, SV0, SD0 = SM["sx0"], SM["st0"], SM["sv0"], SM["sd0"]
        SD0b, SX0b = SM["sd0b"], SM["sx0b"]
        with ExitStack() as p0:
            e0 = p0.enter_context
            p0x = [e0(nc.sbuf_tensor(f"p0x{i}", [128, D], F32)) for i in range(2)]
            p0f = [e0(nc.sbuf_tensor(f"p0f{i}", [128, 128], F32)) for i in range(2)]
            p0b = [e0(nc.sbuf_tensor(f"p0b{i}", [128, 128], BF16)) for i in range(2)]
            ps0 = [e0(nc.psum_tensor(f"ps0{i}", [128, 512], F32)) for i in range(2)]

            for tt in range(8):
                @blk.scalar
                def _(q, tt=tt):
                    if tt >= 2:
                        ST0.wait(q, (tt - 1) * 32)
                    (SX0 if tt % 2 == 0 else SX0b).inc(
                        q.dma_start(out=p0x[tt % 2][:],
                                    in_=x_slice[tt * 128:(tt + 1) * 128, :]), 16)

                for dc in range(32):
                    k = tt * 32 + dc

                    @blk.tensor
                    def _(te, tt=tt, dc=dc, k=k):
                        if k == 0:
                            SC.wait(te)
                        (SX0 if tt % 2 == 0 else SX0b).wait(te, (tt // 2 + 1) * 16)
                        if k >= 2:
                            SV0.wait(te, 2 * (k - 1))
                        ST0.inc(te.transpose(ps0[k % 2][:, 0:128],
                                             p0x[tt % 2][:, dc * 128:(dc + 1) * 128],
                                             cs["IDF"][:]))

                    @blk.vector
                    def _(v, k=k):
                        ST0.wait(v)
                        if k >= 2:
                            sdp = SD0 if (k % 2 == 0) else SD0b
                            sdp.wait(v, (k // 2) * 32)
                        v.tensor_copy(out=p0f[k % 2][:], in_=ps0[k % 2][:, 0:128])
                        SV0.inc(v.tensor_copy(out=p0b[k % 2][:], in_=ps0[k % 2][:, 0:128]), 2)

                    @blk.sync
                    def _(q, tt=tt, dc=dc, k=k):
                        SV0.wait(q)
                        sdp = SD0 if (k % 2 == 0) else SD0b
                        sdp.inc(q.dma_start(
                            out=xTf32[dc * 128:(dc + 1) * 128, tt * 128:(tt + 1) * 128],
                            in_=p0f[k % 2][:]), 16)
                        sdp.inc(q.dma_start(
                            out=xTc_in[dc * 128:(dc + 1) * 128, tt * 128:(tt + 1) * 128],
                            in_=p0b[k % 2][:]), 16)

        SG1 = SM["sg1"]

        @blk.gpsimd
        def _(q):
            SD0.wait(q)
            SD0b.wait(q)
            if _SKIP_COLLECTIVES:
                q.dma_start(out=xg[0:D, :], in_=xTc_in[:]).then_inc(SG1.h, 16)
                SG1.v += 16
                q.wait_ge(SG1.h, SG1.v)
            else:
                SG1.inc(q.collective_compute(
                    "AllGather", ALU.bypass,
                    replica_groups=[list(range(NCORES))],
                    ins=[xTc_in[:].opt()],
                    outs=[xg[:].opt()],
                ))
                q.wait_ge(SG1.h, SG1.v)

        SSEL = SM["ssel"]
        selstack = ExitStack()
        psmS = selstack.enter_context(nc.psum_tensor("psmS", [16, 4], F32))

        @blk.vector
        def _(v):
            SC.wait(v)
            v.tensor_copy(out=act_f[:], in_=act_i[:])
            SSEL.inc(v.drain())
            SSEL.inc(v.tensor_tensor(out=sel4[:], in0=cs["IOTA4r"][:],
                                     in1=act_f[:].to_broadcast([4, 16]),
                                     op=ALU.is_equal))

        @blk.tensor
        def _(te):
            SSEL.wait(te)
            SSEL.inc(te.transpose(psmS[:], sel4[:], cs["IDF"][0:4, 0:4]))

        @blk.vector
        def _(v):
            SSEL.wait(v)
            SSEL.inc(v.tensor_copy(out=selT[:], in_=psmS[:]))
        selstack.close()

        if _NPHASE < 2:
            @blk.sync
            def _(q):
                SG1.wait(q)
                SSEL.wait(q)
            return nc
        SXG, SRP, SMM, SEV = SM["sxg"], SM["srp"], SM["smm"], SM["sev"]
        SRPb, SXGb = SM["srpb"], SM["sxgb"]
        SQDS = [SM["sqd"], SM["sqd1"], SM["sqd2"], SM["sqd3"]]
        SVT, SVE, SQD = SM["svt"], SM["sve"], SM["sqd"]
        ROPES = ["ropeCq", "ropeSq", "ropeC", "ropeS"]
        with ExitStack() as p1:
            e1 = p1.enter_context
            wq_sb = [e1(nc.sbuf_tensor(f"wq_sb{i}", [128, 32 * 128], BF16))
                     for i in range(4)]
            wk_sb = e1(nc.sbuf_tensor("wk_sb", [128, 32 * 128], BF16))
            wv_sb = e1(nc.sbuf_tensor("wv_sb", [128, 32 * 128], BF16))

            @blk.gpsimd
            def _(q):
                for qc in range(4):
                    SW.inc(q.dma_start(
                        out=wq_sb[qc][:].rearrange("p (j c) -> p j c", c=128),
                        in_=wqT[:, qc * 128:(qc + 1) * 128]
                            .rearrange("(j p) c -> p j c", p=128)), 16)
                SW.inc(q.dma_start(
                    out=wk_sb[:].rearrange("p (j c) -> p j c", c=128),
                    in_=wkT[:].rearrange("(j p) c -> p j c", p=128)), 16)
                SW.inc(q.dma_start(
                    out=wv_sb[:].rearrange("p (j c) -> p j c", c=128),
                    in_=wvT[:].rearrange("(j p) c -> p j c", p=128)), 16)

            xgt = [e1(nc.sbuf_tensor(f"xgt{i}", [128, 32 * 512], BF16)) for i in range(2)]
            rsl = [[e1(nc.sbuf_tensor(f"rsl{i}_{j}", [128, 512], F32))
                    for j in range(4)] for i in range(2)]
            shf = e1(nc.sbuf_tensor("shf", [128, 512], F32))
            tb1 = e1(nc.sbuf_tensor("tb1", [128, 512], F32))
            tb2 = e1(nc.sbuf_tensor("tb2", [128, 512], F32))
            qrb = [e1(nc.sbuf_tensor(f"qrb{i}", [128, 512], BF16)) for i in range(4)]
            vsb = e1(nc.sbuf_tensor("vsb", [128, 512], BF16))
            ps1p = [e1(nc.psum_tensor(f"ps1p{i}", [128, 512], F32)) for i in range(2)]
            psvt = [e1(nc.psum_tensor(f"psvt{i}", [128, 1024], BF16)) for i in range(2)]

            for tt in range(16):
                r, cof = tt // 2, (tt % 2) * 512
                spos = (tt % 8) * 512

                @blk.sync
                def _(q, tt=tt, r=r, cof=cof):
                    if tt == 0:
                        SG1.wait(q)
                    if tt >= 2:
                        SMM.wait(q, (tt - 1) * 6)
                    sxg = SXG if (tt % 2 == 0) else SXGb
                    for j in range(32):
                        sxg.inc(q.dma_start(
                            out=xgt[tt % 2][:, j * 512:(j + 1) * 512],
                            in_=xg[r * D + j * 128: r * D + (j + 1) * 128,
                                   cof:cof + 512]), 16)

                @blk.scalar
                def _(q, tt=tt, spos=spos):
                    if tt == 0:
                        SC.wait(q)
                    if tt >= 2:
                        SEV.wait(q, (tt - 1) * 6)
                    srp = SRP if (tt % 2 == 0) else SRPb
                    for j, nm in enumerate(ROPES):
                        srp.inc(q.dma_start(out=rsl[tt % 2][j][:],
                                            in_=cdram[nm][:, spos:spos + 512]), 16)

                for oc in range(6):
                    u = tt * 6 + oc

                    @blk.tensor
                    def _(te, tt=tt, oc=oc, u=u):
                        if u == 0:
                            SW.wait(te)
                        (SXG if (tt % 2 == 0) else SXGb).wait(
                            te, (tt // 2 + 1) * 32 * 16)
                        if u >= 2:
                            SEV.wait(te, u - 1)
                        w = wq_sb[oc] if oc < 4 else (wk_sb if oc == 4 else wv_sb)
                        for j in range(32):
                            ins = te.matmul(ps1p[u % 2][:],
                                            w[:, j * 128:(j + 1) * 128],
                                            xgt[tt % 2][:, j * 512:(j + 1) * 512],
                                            start=(j == 0), stop=(j == 31))
                        SMM.inc(ins)

                    @blk.vector
                    def _(v, tt=tt, oc=oc, u=u):
                        SMM.wait(v)
                        (SRP if (tt % 2 == 0) else SRPb).wait(v, (tt // 2 + 1) * 64)
                        pp = ps1p[u % 2]
                        if oc < 5:
                            ci = 0 if oc < 4 else 2
                            v.tensor_copy(out=shf[0:64, :], in_=pp[64:128, :])
                            v.tensor_copy(out=shf[64:128, :], in_=pp[0:64, :])
                            v.tensor_tensor(out=tb1[:], in0=pp[:],
                                            in1=rsl[tt % 2][ci][:], op=ALU.mult)
                            v.tensor_tensor(out=tb2[:], in0=shf[:],
                                            in1=rsl[tt % 2][ci + 1][:], op=ALU.mult)
                            if oc < 4:
                                if tt >= 1:
                                    SQDS[oc].wait(v, tt * 16)
                                SEV.inc(v.tensor_tensor(out=qrb[oc][:], in0=tb1[:],
                                                        in1=tb2[:], op=ALU.add))
                            else:
                                SEV.inc(v.tensor_tensor(
                                    out=KT[:, tt * 512:(tt + 1) * 512],
                                    in0=tb1[:], in1=tb2[:], op=ALU.add))
                        else:
                            if tt >= 1:
                                SVT.wait(v, tt * 4)
                            SEV.inc(v.tensor_copy(out=vsb[:], in_=pp[:]))

                    if oc < 4:
                        @blk.sync
                        def _(q, tt=tt, oc=oc):
                            SEV.wait(q)
                            SQDS[oc].inc(q.dma_start(
                                out=qTd[oc * 128:(oc + 1) * 128,
                                        tt * 512:(tt + 1) * 512],
                                in_=qrb[oc][:]), 16)

                for i in range(4):
                    k = tt * 4 + i

                    @blk.tensor
                    def _(te, tt=tt, i=i, k=k):
                        SEV.wait(te)
                        if k >= 2:
                            SVE.wait(te, k - 1)
                        SVT.inc(te.transpose(psvt[k % 2][:, 0:128],
                                             vsb[:, i * 128:(i + 1) * 128],
                                             cs["IDB"][:]))

                    @blk.vector
                    def _(v, k=k):
                        SVT.wait(v)
                        SVE.inc(v.tensor_copy(out=VT[:, k * 128:(k + 1) * 128],
                                              in_=psvt[k % 2][:, 0:128]))

        if _NPHASE < 3:
            @blk.sync
            def _(q):
                SQD.wait(q)
                SEV.wait(q)
                SVE.wait(q)
            return nc
        SQH, SS1, SSE, SMD, SCN = (SM["sqh"], SM["ss1"], SM["sse"],
                                   SM["smd"], SM["scn"])
        SFIN, SPM, SPN, SPT, SPE = (SM["sfin"], SM["spm"], SM["spn"],
                                    SM["spt"], SM["spe"])
        SAV, SOE, SOS, SB = SM["sav"], SM["soe"], SM["sos"], SM["sb"]
        with ExitStack() as p2:
            e2 = p2.enter_context
            QhT = [e2(nc.sbuf_tensor(f"QhT{i}", [128, S], BF16)) for i in range(2)]
            s4 = e2(nc.sbuf_tensor("s4", [128, 4 * S], F32))
            junkb = e2(nc.sbuf_tensor("junkb", [128, S], BF16))
            bband = e2(nc.sbuf_tensor("bband", [128, S], BF16))
            bmsm = e2(nc.sbuf_tensor("bmsm", [128, S], F32))
            pm = e2(nc.sbuf_tensor("pm", [128, S], BF16))
            pmn = e2(nc.sbuf_tensor("pmn", [128, S], BF16))
            pmT = e2(nc.sbuf_tensor("pmT", [128, 32 * 512], BF16))
            obf = e2(nc.sbuf_tensor("obf", [128, 512], BF16))
            Mall = e2(nc.sbuf_tensor("Mall", [128, 32], F32))
            Tall = e2(nc.sbuf_tensor("Tall", [128, 32], F32))
            Tsh = e2(nc.sbuf_tensor("Tsh", [128, 32], F32))
            Zall = e2(nc.sbuf_tensor("Zall", [128, 32], F32))
            rZ = e2(nc.sbuf_tensor("rZ", [128, 32], F32))
            loc = e2(nc.sbuf_tensor("loc", [128, 4], F32))
            midc = e2(nc.sbuf_tensor("midc", [128, 4], F32))
            hic = e2(nc.sbuf_tensor("hic", [128, 4], F32))
            cnt4 = e2(nc.sbuf_tensor("cnt4", [128, 4], F32))
            g4 = e2(nc.sbuf_tensor("g4", [128, 4], F32))
            sh4 = e2(nc.sbuf_tensor("sh4", [128, 4], F32))
            j4 = e2(nc.sbuf_tensor("j4", [128, 4], F32))
            m48 = e2(nc.sbuf_tensor("m48", [128, 4], F32))
            e8t = e2(nc.sbuf_tensor("e8t", [128, 8], F32))
            jm8 = e2(nc.sbuf_tensor("jm8", [128, 8], F32))
            me8 = e2(nc.sbuf_tensor("me8", [128, 8], F32))
            colA = e2(nc.sbuf_tensor("colA", [128, 1], F32))
            colB = e2(nc.sbuf_tensor("colB", [128, 1], F32))
            colC = e2(nc.sbuf_tensor("colC", [128, 1], F32))
            nbc = e2(nc.sbuf_tensor("nbc", [128, 1], F32))
            ps2a = [e2(nc.psum_tensor(f"ps2a{i}", [128, 512], F32)) for i in range(2)]
            psqt = [e2(nc.psum_tensor(f"psqt{i}", [128, 1024], BF16)) for i in range(2)]
            pso = e2(nc.psum_tensor("pso", [128, 512], F32))

            s1cnt = [0]
            ptcnt = [0]

            for bh in range(_P2_BH):
                b, hl = bh // 4, bh % 4

                @blk.sync
                def _(q, bh=bh, b=b, hl=hl):
                    if bh == 0:
                        for sq_ in SQDS:
                            sq_.wait(q)
                    if bh >= 2:
                        SAV.wait(q, (bh - 1) * 8)
                    SQH.inc(q.dma_start(
                        out=QhT[bh % 2][:],
                        in_=qTd[hl * 128:(hl + 1) * 128, b * S:(b + 1) * S]), 16)

                for qc in range(_P2_QC0, _P2_QC):
                    u = bh * 8 + qc
                    first_of_unit = [True]
                    for jb in range(4):
                        jq = qc * 4 + jb
                        W = (jq + 1) * 128
                        nkt = (W + 511) // 512
                        for kt in range(nkt):
                            n = min(512, W - kt * 512)
                            ki = s1cnt[0]
                            s1cnt[0] += 1
                            fou = first_of_unit[0]
                            first_of_unit[0] = False

                            @blk.tensor
                            def _(te, bh=bh, b=b, qc=qc, jb=jb, kt=kt, n=n,
                                  ki=ki, nkt=nkt, W=W, fou=fou):
                                if ki == 0:
                                    SEV.wait(te)
                                    SVE.wait(te)
                                if fou and SOE.v >= 1:
                                    SOE.wait(te, SOE.v)
                                SQH.wait(te)
                                if ki >= 2:
                                    SSE.wait(te, ki - 1)
                                has_diag = (kt == nkt - 1)
                                te.matmul(
                                    ps2a[ki % 2][:, 0:n],
                                    QhT[bh % 2][:, qc * 512 + jb * 128:
                                                 qc * 512 + jb * 128 + 128],
                                    KT[:, b * S + kt * 512: b * S + kt * 512 + n],
                                    start=True, stop=not has_diag)
                                if has_diag:
                                    ti = n // 128 - 1
                                    SS1.inc(te.matmul(
                                        ps2a[ki % 2][:, 0:n], cs["IDB"][:],
                                        cs["TRIP"][:, ti * 512: ti * 512 + n],
                                        start=False, stop=True))
                                else:
                                    SS1.inc(te.nop())

                            @blk.scalar
                            def _(sq, jb=jb, kt=kt, n=n, ki=ki, fou=fou):
                                if fou and SAV.v >= 1:
                                    SAV.wait(sq, SAV.v)
                                SS1.wait(sq)
                                SSE.inc(sq.copy(
                                    out=s4[:, jb * S + kt * 512:
                                           jb * S + kt * 512 + n],
                                    in_=ps2a[ki % 2][:, 0:n]))

                    if _P2_STAGE < 2:
                        continue
                    @blk.vector
                    def _(v, qc=qc):
                        SSE.wait(v)
                        for jb in range(4):
                            jq = qc * 4 + jb
                            W = (jq + 1) * 128
                            ins = v.tensor_scalar(out=junkb[:, 0:W],
                                                  in0=s4[:, jb * S: jb * S + W],
                                                  scalar1=-1.0e30, scalar2=None,
                                                  op0=ALU.max, op1=ALU.max,
                                                  accum_out=Mall[:, jq:jq + 1])
                        SB.inc(ins)
                        SB.wait(v)
                        v.tensor_scalar(out=loc[:], in0=Mall[:, qc * 4:qc * 4 + 4],
                                        scalar1=BIS_W0, scalar2=None,
                                        op0=ALU.subtract)

                    for r in range(N_ROUNDS):
                        w = BIS_W0 / (2.0 ** r)

                        @blk.vector
                        def _(v, r=r, w=w, qc=qc):
                            SCN.wait(v)
                            if r == 0:
                                v.tensor_scalar(out=midc[:],
                                                in0=Mall[:, qc * 4:qc * 4 + 4],
                                                scalar1=BIS_W0 / 2, scalar2=None,
                                                op0=ALU.subtract)
                            else:
                                v.drain()
                                v.tensor_scalar(out=midc[:], in0=loc[:],
                                                scalar1=w / 2, scalar2=None,
                                                op0=ALU.add)
                            SMD.inc(v.drain())

                        if r % 2 == 0:
                            @blk.scalar
                            def _(sq, qc=qc):
                                SMD.wait(sq)
                                for jb in range(4):
                                    W = (qc * 4 + jb + 1) * 128
                                    ins = sq.activation(
                                        out=junkb[:, 0:W],
                                        in_=s4[:, jb * S: jb * S + W],
                                        func=AF.Sign, bias=midc[:, jb:jb + 1],
                                        scale=-1.0,
                                        accum_out=cnt4[:, jb:jb + 1])
                                SCN.inc(ins)

                            @blk.vector
                            def _(v, qc=qc, w=w):
                                SCN.wait(v)
                                v.tensor_tensor(out=g4[:], in0=cnt4[:],
                                                in1=cs["WH820"][:, qc * 4:qc * 4 + 4],
                                                op=ALU.is_le)
                                v.drain()
                                v.scalar_tensor_tensor(out=loc[:], in0=g4[:],
                                                       scalar=w / 2, in1=loc[:],
                                                       op0=ALU.mult, op1=ALU.add)
                        else:
                            @blk.vector
                            def _(v, qc=qc, w=w):
                                SMD.wait(v)
                                for jb in range(4):
                                    W = (qc * 4 + jb + 1) * 128
                                    ins = v.tensor_scalar(
                                        out=junkb[:, 0:W],
                                        in0=s4[:, jb * S: jb * S + W],
                                        scalar1=midc[:, jb:jb + 1],
                                        scalar2=None,
                                        op0=ALU.is_ge, op1=ALU.add,
                                        accum_out=cnt4[:, jb:jb + 1])
                                SB.inc(ins)
                                SB.wait(v)
                                v.tensor_scalar(out=g4[:], in0=cnt4[:],
                                                scalar1=float(KREM) - 0.25,
                                                scalar2=None, op0=ALU.is_ge)
                                v.drain()
                                v.scalar_tensor_tensor(out=loc[:], in0=g4[:],
                                                       scalar=w / 2, in1=loc[:],
                                                       op0=ALU.mult, op1=ALU.add)

                    if _P2_STAGE < 3:
                        continue
                    @blk.vector
                    def _(v, qc=qc):
                        SMD.wait(v)
                        wl = BIS_W0 / (2.0 ** N_ROUNDS)
                        v.drain()
                        v.tensor_scalar(out=hic[:], in0=loc[:], scalar1=wl,
                                        scalar2=None, op0=ALU.add)
                        v.drain()
                        for jb in range(4):
                            W = (qc * 4 + jb + 1) * 128
                            ins = v.tensor_scalar(out=junkb[:, 0:W],
                                                  in0=s4[:, jb * S: jb * S + W],
                                                  scalar1=hic[:, jb:jb + 1],
                                                  scalar2=None,
                                                  op0=ALU.is_ge, op1=ALU.add,
                                                  accum_out=cnt4[:, jb:jb + 1])
                        SB.inc(ins)
                        SB.wait(v)
                        v.tensor_scalar(out=j4[:], in0=cnt4[:], scalar1=float(KREM),
                                        scalar2=-1.0, op0=ALU.subtract, op1=ALU.mult)
                        v.tensor_scalar(out=sh4[:], in0=loc[:], scalar1=-1.0,
                                        scalar2=1.0, op0=ALU.mult, op1=ALU.add)
                        v.tensor_scalar(out=m48[:], in0=Mall[:, qc * 4:qc * 4 + 4],
                                        scalar1=BIS_W0, scalar2=None,
                                        op0=ALU.subtract)

                    @blk.vector
                    def _(v, qc=qc):
                        if SAV.v >= 1:
                            SAV.wait(v, SAV.v)
                        for kt in range(4 * qc + 4):
                            for jb in range(max(0, kt - 4 * qc)):
                                v.memset(pmT[:, kt * 512 + jb * 128:
                                             kt * 512 + jb * 128 + 128], 0.0)
                        v.nop()

                    for jb in range(4):
                        jq = qc * 4 + jb
                        W = (jq + 1) * 128

                        @blk.vector
                        def _(v, jb=jb, jq=jq, W=W):
                            SPM.wait(v)
                            sblk = s4[:, jb * S: jb * S + W]
                            v.tensor_scalar(out=junkb[:, 0:W], in0=sblk,
                                            scalar1=loc[:, jb:jb + 1], scalar2=None,
                                            op0=ALU.is_ge)
                            v.tensor_scalar(out=bband[:, 0:W], in0=sblk,
                                            scalar1=hic[:, jb:jb + 1], scalar2=None,
                                            op0=ALU.is_lt)
                            v.tensor_tensor(out=bband[:, 0:W], in0=junkb[:, 0:W],
                                            in1=bband[:, 0:W], op=ALU.mult)
                            v.scalar_tensor_tensor(out=bmsm[:, 0:W], in0=sblk,
                                                   scalar=sh4[:, jb:jb + 1],
                                                   in1=bband[:, 0:W],
                                                   op0=ALU.add, op1=ALU.mult)
                            v.max(out=e8t[:], in_=bmsm[:, 0:W])
                            v.tensor_scalar(out=colA[:], in0=j4[:, jb:jb + 1],
                                            scalar1=1.0, scalar2=None,
                                            op0=ALU.subtract)
                            v.drain()
                            v.tensor_tensor(out=jm8[:], in0=cs["IOTA8"][:],
                                            in1=colA[:].to_broadcast([128, 8]),
                                            op=ALU.is_equal)
                            v.drain()
                            v.scalar_tensor_tensor(out=me8[:], in0=e8t[:],
                                                   scalar=1.0, in1=jm8[:],
                                                   op0=ALU.mult, op1=ALU.mult,
                                                   accum_out=colB[:])
                            SB.inc(v.tensor_scalar(out=jm8[:], in0=jm8[:],
                                                   scalar1=1.0, scalar2=None,
                                                   op0=ALU.mult, op1=ALU.add,
                                                   accum_out=colC[:]))
                            SB.wait(v)
                            v.tensor_tensor(out=colC[:], in0=colC[:],
                                            in1=cs["KAM"][:, jq:jq + 1],
                                            op=ALU.mult)
                            v.drain()
                            v.tensor_tensor(out=colA[:], in0=sh4[:, jb:jb + 1],
                                            in1=colC[:], op=ALU.mult)
                            v.drain()
                            v.tensor_tensor(out=colB[:], in0=colB[:], in1=colA[:],
                                            op=ALU.subtract)
                            v.tensor_scalar(out=colA[:], in0=colC[:],
                                            scalar1=1.0e30, scalar2=-1.0e30,
                                            op0=ALU.mult, op1=ALU.add)
                            v.drain()
                            v.tensor_tensor(out=Tall[:, jq:jq + 1], in0=colB[:],
                                            in1=colA[:], op=ALU.add)
                            v.drain()
                            v.tensor_tensor(out=Tsh[:, jq:jq + 1],
                                            in0=Tall[:, jq:jq + 1],
                                            in1=m48[:, jb:jb + 1], op=ALU.max)
                            v.drain()
                            v.tensor_scalar(out=junkb[:, 0:W], in0=sblk,
                                            scalar1=Tall[:, jq:jq + 1], scalar2=None,
                                            op0=ALU.is_ge)
                            v.scalar_tensor_tensor(out=bmsm[:, 0:W], in0=sblk,
                                                   scalar=BIGP, in1=junkb[:, 0:W],
                                                   op0=ALU.add, op1=ALU.mult)
                            SFIN.inc(v.tensor_scalar(out=nbc[:],
                                                     in0=Tsh[:, jq:jq + 1],
                                                     scalar1=BIGP, scalar2=-1.0,
                                                     op0=ALU.add, op1=ALU.mult))

                        @blk.scalar
                        def _(sq, jq=jq, W=W):
                            SFIN.wait(sq)
                            SPN.wait(sq)
                            SPM.inc(sq.activation(out=pm[:, 0:W], in_=bmsm[:, 0:W],
                                                  func=AF.Exp, bias=nbc[:],
                                                  scale=1.0,
                                                  accum_out=Zall[:, jq:jq + 1]))

                        @blk.vector
                        def _(v, jq=jq, W=W):
                            SPM.wait(v)
                            v.reciprocal(out=rZ[:, jq:jq + 1], in_=Zall[:, jq:jq + 1])
                            v.drain()
                            SPN.inc(v.tensor_scalar(out=pmn[:, 0:W], in0=pm[:, 0:W],
                                                    scalar1=rZ[:, jq:jq + 1],
                                                    scalar2=None, op0=ALU.mult))

                        for kt in range(W // 128 if _P2_STAGE >= 4 else 0):
                            k2 = ptcnt[0]
                            ptcnt[0] += 1

                            @blk.tensor
                            def _(te, kt=kt, k2=k2):
                                SPN.wait(te)
                                if k2 >= 2:
                                    SPE.wait(te, k2 - 1)
                                SPT.inc(te.transpose(
                                    psqt[k2 % 2][:, 0:128],
                                    pmn[:, kt * 128:(kt + 1) * 128], cs["IDB"][:]))

                            @blk.vector
                            def _(v, kt=kt, jb=jb, k2=k2):
                                SPT.wait(v)
                                SPE.inc(v.tensor_copy(
                                    out=pmT[:, kt * 512 + jb * 128:
                                            kt * 512 + jb * 128 + 128],
                                    in_=psqt[k2 % 2][:, 0:128]))

                    if _P2_STAGE < 5:
                        continue
                    @blk.tensor
                    def _(te, b=b, qc=qc, u=u):
                        SPE.wait(te)
                        if SOE.v >= 1 and _P2_STAGE >= 6:
                            SOE.wait(te, SOE.v)
                        nktc = 4 * qc + 4
                        for kt in range(nktc):
                            ins = te.matmul(
                                pso[:, 0:512],
                                VT[:, (b * 32 + kt) * 128:(b * 32 + kt + 1) * 128],
                                pmT[:, kt * 512: kt * 512 + 512],
                                start=(kt == 0), stop=(kt == nktc - 1))
                        SAV.inc(ins)

                    if _P2_STAGE < 6:
                        continue
                    @blk.vector
                    def _(v, u=u):
                        SAV.wait(v)
                        if SOS.v >= 16:
                            SOS.wait(v, SOS.v)
                        SOE.inc(v.tensor_copy(out=obf[:], in_=pso[:]))

                    @blk.sync
                    def _(q, b=b, hl=hl, qc=qc):
                        SOE.wait(q)
                        SOS.inc(q.dma_start(
                            out=oc_in[hl * 128:(hl + 1) * 128,
                                      b * S + qc * 512: b * S + (qc + 1) * 512],
                            in_=obf[:]), 16)

        if _NPHASE < 4:
            @blk.sync
            def _(q):
                SOS.wait(q)
            return nc
        SG2 = SM["sg2"]

        @blk.gpsimd
        def _(q):
            SOS.wait(q)
            if _SKIP_COLLECTIVES:
                q.dma_start(out=og[0:512, :], in_=oc_in[:]).then_inc(SG2.h, 16)
                SG2.v += 16
                q.wait_ge(SG2.h, SG2.v)
            else:
                SG2.inc(q.collective_compute(
                    "AllGather", ALU.bypass,
                    replica_groups=[list(range(NCORES))],
                    ins=[oc_in[:].opt()],
                    outs=[og[:].opt()],
                ))
                q.wait_ge(SG2.h, SG2.v)

        (SOGL, SFW, SXT, SMW, SR1E, SST, SRL, SX2, SAL, SR2E, SYT, SYE,
         SYD) = (SM["sogl"], SM["sfw"], SM["sxt"], SM["smw"], SM["sr1e"],
                 SM["sst"], SM["srl"], SM["sx2"], SM["sal"], SM["sr2e"],
                 SM["syt"], SM["sye"], SM["syd"])
        SRLl, SRLlb = SM["srll"], SM["srllb"]
        SXTb = SM["sxtb"]
        SW2 = [SM["sw2a"], SM["sw2b"]]
        with ExitStack() as p3:
            e3 = p3.enter_context
            og_sb = e3(nc.sbuf_tensor("og_sb", [128, 32 * 512], BF16))
            x2bf = e3(nc.sbuf_tensor("x2bf", [128, 32 * 512], BF16))
            ws2 = [e3(nc.sbuf_tensor(f"ws2_{i}", [128, 32 * 128], BF16))
                   for i in range(2)]
            aall_sb = e3(nc.sbuf_tensor("aall_sb", [128, 32 * 128], BF16))
            ball_sb = e3(nc.sbuf_tensor("ball_sb", [128, 32 * 128], BF16))
            rt_sb = e3(nc.sbuf_tensor("rt_sb", [128, 32 * 16], BF16))
            xtf = [e3(nc.sbuf_tensor(f"xtf{i}", [128, 512], F32)) for i in range(2)]
            r1c = [e3(nc.sbuf_tensor(f"r1c{i}", [128, 512], F32)) for i in range(2)]
            sqs = [e3(nc.sbuf_tensor(f"sqs{i}", [128, 512], F32)) for i in range(2)]
            rlc = [e3(nc.sbuf_tensor(f"rlc{i}", [128, 512], F32)) for i in range(2)]
            x2c = e3(nc.sbuf_tensor("x2c", [128, 512], F32))
            scr = e3(nc.sbuf_tensor("scr", [128, 512], F32))
            aev = e3(nc.sbuf_tensor("aev", [128, 512], F32))
            bev = e3(nc.sbuf_tensor("bev", [128, 512], F32))
            ztbf = e3(nc.sbuf_tensor("ztbf", [128, 512], BF16))
            lsb = e3(nc.sbuf_tensor("lsb", [16, 512], F32))
            e4sb = e3(nc.sbuf_tensor("e4sb", [4, 512], F32))
            asb = e3(nc.sbuf_tensor("asb", [4, 512], F32))
            aasb = e3(nc.sbuf_tensor("aasb", [16, 512], F32))
            arsb = e3(nc.sbuf_tensor("arsb", [128, 512], F32))
            lp4 = e3(nc.sbuf_tensor("lp4", [4, 1], F32))
            rowA = e3(nc.sbuf_tensor("rowA", [1, 512], F32))
            rowB = e3(nc.sbuf_tensor("rowB", [1, 512], F32))
            rowC = e3(nc.sbuf_tensor("rowC", [1, 512], F32))
            rowD = e3(nc.sbuf_tensor("rowD", [1, 512], F32))
            ysb = [e3(nc.sbuf_tensor(f"ysb{i}", [128, 128], F32)) for i in range(2)]
            py = [e3(nc.psum_tensor(f"py{i}", [128, 512], F32)) for i in range(2)]
            stA = e3(nc.psum_tensor("stA", [1, 512], F32))
            stB = e3(nc.psum_tensor("stB", [1, 512], F32))
            psz = e3(nc.psum_tensor("psz", [128, 512], F32))
            pscr = e3(nc.psum_tensor("pscr", [128, 512], F32))
            ptr = [e3(nc.psum_tensor(f"ptr{i}", [128, 512], F32)) for i in range(2)]

            @blk.gpsimd
            def _(q):
                SC.wait(q)
                SFW.inc(q.dma_start(
                    out=aall_sb[:].rearrange("p (j c) -> p j c", c=128),
                    in_=aallT[:].rearrange("(j p) c -> p j c", p=128)), 16)
                SFW.inc(q.dma_start(
                    out=rt_sb[:].rearrange("p (j c) -> p j c", c=16),
                    in_=routerT[:].rearrange("(j p) c -> p j c", p=128)), 16)
                SFW.inc(q.dma_start(out=ball_sb[:], in_=ball_d[:]), 16)

            def ln_rows(tag):
                @blk.vector
                def _(v):
                    SST.wait(v)
                    v.tensor_scalar(out=rowA[:], in0=stA[0:1, :], scalar1=1.0 / D,
                                    scalar2=None, op0=ALU.mult)
                    v.tensor_scalar(out=rowB[:], in0=stB[0:1, :], scalar1=1.0 / D,
                                    scalar2=None, op0=ALU.mult)
                    v.tensor_tensor(out=rowC[:], in0=rowA[:], in1=rowA[:],
                                    op=ALU.mult)
                    v.tensor_tensor(out=rowC[:], in0=rowB[:], in1=rowC[:],
                                    op=ALU.subtract)
                    SST.inc(v.tensor_scalar(out=rowC[:], in0=rowC[:],
                                            scalar1=LN_EPS, scalar2=None,
                                            op0=ALU.add))

                @blk.scalar
                def _(sq):
                    SST.wait(sq)
                    sq.activation(out=rowD[:], in_=rowC[:], func=AF.Ln,
                                  bias=0.0, scale=1.0)
                    SST.inc(sq.activation(out=rowD[:], in_=rowD[:], func=AF.Exp,
                                          bias=0.0, scale=-0.5))

                @blk.vector
                def _(v):
                    SST.wait(v)
                    SST.inc(v.tensor_tensor(out=rowB[:], in0=rowA[:], in1=rowD[:],
                                            op=ALU.mult))

                @blk.tensor
                def _(te):
                    SST.wait(te)
                    te.matmul(pscr[:, 0:512], cs["ONer"][:], rowD[:],
                              start=True, stop=True)
                    SST.inc(te.matmul(py[0][:], cs["ONer"][:], rowB[:],
                                      start=True, stop=True))

                @blk.vector
                def _(v):
                    SST.wait(v)
                    v.tensor_copy(out=aev[:], in_=pscr[:, 0:512])
                    SST.inc(v.tensor_copy(out=bev[:], in_=py[0][:]))

            for h2 in range(2):
                hof = h2 * 512

                @blk.sync
                def _(q, h2=h2, hof=hof):
                    if h2 == 0:
                        SG2.wait(q)
                    else:
                        SMW.wait(q)
                    if _SKIP_SWITCH:
                        base = hof
                        for j in range(32):
                            q.dma_start(
                                out=og_sb[:, j * 512:(j + 1) * 512],
                                in_=og[j * 128:(j + 1) * 128, base:base + 512]
                            ).then_inc(SOGL.h, 16)
                    else:
                        for arm in q.switch_core_id(n=128):
                            c = int(arm.logical) % 8
                            base = c * TPC + hof
                            for j in range(32):
                                q.dma_start(
                                    out=og_sb[:, j * 512:(j + 1) * 512],
                                    in_=og[j * 128:(j + 1) * 128, base:base + 512]
                                ).then_inc(SOGL.h, 16)
                    SOGL.v += 32 * 16

                smw_base = SMW.v
                sr1_base = SR1E.v
                for oc in range(32):
                    @blk.gpsimd
                    def _(q, oc=oc):
                        if oc >= 2:
                            SMW.wait(q, smw_base + oc - 1)
                        elif smw_base > 0:
                            SMW.wait(q, smw_base)
                        SW2[oc % 2].inc(q.dma_start(
                            out=ws2[oc % 2][:].rearrange("p (j c) -> p j c", c=128),
                            in_=woT[oc].rearrange("(j p) c -> p j c", p=128)), 16)

                    @blk.scalar
                    def _(sq, oc=oc, hof=hof):
                        if oc >= 2:
                            SR1E.wait(sq, sr1_base + 2 * (oc - 2) + 1)
                        elif sr1_base > 0:
                            SR1E.wait(sq, sr1_base)
                        (SXT if oc % 2 == 0 else SXTb).inc(
                            sq.dma_start(out=xtf[oc % 2][:],
                                         in_=xTf32[oc * 128:(oc + 1) * 128,
                                                   hof:hof + 512]), 16)

                    @blk.tensor
                    def _(te, oc=oc):
                        SOGL.wait(te)
                        SFW.wait(te)
                        SW2[oc % 2].wait(te)
                        if oc >= 2:
                            SR1E.wait(te, sr1_base + 2 * (oc - 2) + 1)
                        for j in range(32):
                            ins = te.matmul(py[oc % 2][:],
                                            ws2[oc % 2][:, j * 128:(j + 1) * 128],
                                            og_sb[:, j * 512:(j + 1) * 512],
                                            start=(j == 0), stop=(j == 31))
                        SMW.inc(ins)

                    @blk.vector
                    def _(v, oc=oc):
                        SMW.wait(v)
                        (SXT if oc % 2 == 0 else SXTb).wait(v)
                        if oc >= 2:
                            SRL.wait(v)
                        SR1E.inc(v.tensor_tensor(out=r1c[oc % 2][:],
                                                 in0=py[oc % 2][:],
                                                 in1=xtf[oc % 2][:], op=ALU.add))

                    @blk.scalar
                    def _(sq, oc=oc):
                        SR1E.wait(sq)
                        SR1E.inc(sq.activation(out=sqs[oc % 2][:],
                                               in_=r1c[oc % 2][:], func=AF.Square))

                    @blk.tensor
                    def _(te, oc=oc):
                        SR1E.wait(te)
                        te.matmul(stA[:], cs["ONESF"][:], r1c[oc % 2][:],
                                  start=(oc == 0), stop=(oc == 31))
                        SST.inc(te.matmul(stB[:], cs["ONESF"][:], sqs[oc % 2][:],
                                          start=(oc == 0), stop=(oc == 31)))

                    @blk.sync
                    def _(q, oc=oc, hof=hof):
                        SST.wait(q)
                        SRL.inc(q.dma_start(
                            out=r1d[oc * 128:(oc + 1) * 128, hof:hof + 512],
                            in_=r1c[oc % 2][:]), 16)

                r1stores = SRL.v
                ln_rows("ln1")

                for oc in range(32):
                    @blk.scalar
                    def _(sq, oc=oc, hof=hof):
                        SST.wait(sq)
                        SRL.wait(sq, r1stores)
                        if oc >= 2:
                            SX2.wait(sq, SX2.v - 1)
                        elif SX2.v > 0:
                            SX2.wait(sq, SX2.v)
                        (SRLl if oc % 2 == 0 else SRLlb).inc(sq.dma_start(
                            out=rlc[oc % 2][:],
                            in_=r1d[oc * 128:(oc + 1) * 128, hof:hof + 512]), 16)

                    @blk.vector
                    def _(v, oc=oc):
                        (SRLl if oc % 2 == 0 else SRLlb).wait(v)
                        v.tensor_tensor(out=scr[:], in0=rlc[oc % 2][:], in1=aev[:],
                                        op=ALU.mult)
                        SX2.inc(v.tensor_tensor(out=x2bf[:, oc * 512:(oc + 1) * 512],
                                                in0=scr[:], in1=bev[:],
                                                op=ALU.subtract))

                @blk.tensor
                def _(te):
                    SX2.wait(te)
                    SFW.wait(te)
                    for j in range(32):
                        ins = te.matmul(psz[:], aall_sb[:, j * 128:(j + 1) * 128],
                                        x2bf[:, j * 512:(j + 1) * 512],
                                        start=(j == 0), stop=(j == 31))
                    SAL.inc(ins)
                    for j in range(32):
                        ins = te.matmul(pscr[0:16, :], rt_sb[:, j * 16:(j + 1) * 16],
                                        x2bf[:, j * 512:(j + 1) * 512],
                                        start=(j == 0), stop=(j == 31))
                    SAL.inc(ins)

                @blk.vector
                def _(v):
                    SAL.wait(v)
                    SAL.inc(v.tensor_scalar(out=lsb[:], in0=pscr[0:16, 0:512],
                                            scalar1=rb_sb[:], scalar2=None,
                                            op0=ALU.add))

                @blk.tensor
                def _(te):
                    SAL.wait(te)
                    SSEL.wait(te)
                    SAL.inc(te.matmul(pscr[0:4, 0:1], selT[:], prior_sb[:],
                                      start=True, stop=True))

                @blk.vector
                def _(v):
                    SAL.wait(v)
                    SAL.inc(v.tensor_scalar(out=lp4[:], in0=pscr[0:4, 0:1],
                                            scalar1=FFN_EPS, scalar2=None,
                                            op0=ALU.add))

                @blk.scalar
                def _(sq):
                    SAL.wait(sq)
                    SAL.inc(sq.activation(out=lp4[:], in_=lp4[:],
                                          func=AF.Ln, bias=0.0, scale=1.0))

                @blk.tensor
                def _(te):
                    SAL.wait(te)
                    SAL.inc(te.matmul(pscr[0:4, 0:512], selT[:], lsb[:],
                                      start=True, stop=True))

                @blk.scalar
                def _(sq):
                    SAL.wait(sq)
                    SAL.inc(sq.activation(out=e4sb[:], in_=pscr[0:4, 0:512],
                                          func=AF.Exp, bias=lp4[:], scale=1.0))

                @blk.tensor
                def _(te):
                    SAL.wait(te)
                    SAL.inc(te.matmul(pscr[0:1, 0:512], cs["ONESF"][0:4, :],
                                      e4sb[:], start=True, stop=True))

                @blk.vector
                def _(v):
                    SAL.wait(v)
                    v.tensor_scalar(out=rowA[:], in0=pscr[0:1, 0:512], scalar1=0.5,
                                    scalar2=None, op0=ALU.mult)
                    SAL.inc(v.reciprocal(out=rowB[:], in_=rowA[:]))

                @blk.tensor
                def _(te):
                    SAL.wait(te)
                    SAL.inc(te.matmul(pscr[0:4, 0:512], cs["ONer"][0:1, 0:4],
                                      rowB[:], start=True, stop=True))

                @blk.vector
                def _(v):
                    SAL.wait(v)
                    SAL.inc(v.tensor_tensor(out=asb[:], in0=e4sb[:],
                                            in1=pscr[0:4, 0:512], op=ALU.mult))

                @blk.tensor
                def _(te):
                    SAL.wait(te)
                    SAL.inc(te.matmul(pscr[0:16, 0:512], sel4[:], asb[:],
                                      start=True, stop=True))

                @blk.vector
                def _(v):
                    SAL.wait(v)
                    SAL.inc(v.tensor_copy(out=aasb[:], in_=pscr[0:16, 0:512]))

                @blk.tensor
                def _(te):
                    SAL.wait(te)
                    SAL.inc(te.matmul(pscr[:, 0:512], cs["E8F"][:], aasb[:],
                                      start=True, stop=True))

                @blk.vector
                def _(v):
                    SAL.wait(v)
                    v.tensor_copy(out=arsb[:], in_=pscr[:, 0:512])
                    SAL.inc(v.tensor_tensor(out=ztbf[:], in0=psz[:], in1=arsb[:],
                                            op=ALU.mult))

                smw2_base = SMW.v
                sr2_base = SR2E.v
                for oc in range(32):
                    @blk.gpsimd
                    def _(q, oc=oc):
                        if oc >= 2:
                            SMW.wait(q, smw2_base + oc - 1)
                        else:
                            SMW.wait(q, smw2_base)
                        SW2[oc % 2].inc(q.dma_start(
                            out=ws2[oc % 2][:].rearrange("p (j c) -> p j c", c=128),
                            in_=wbT[oc].rearrange("(j p) c -> p j c", p=128)), 16)

                    @blk.scalar
                    def _(sq, oc=oc, hof=hof):
                        SAL.wait(sq)
                        SRL.wait(sq, r1stores)
                        if oc >= 2:
                            SX2.wait(sq, SX2.v - 1)
                        else:
                            SX2.wait(sq, SX2.v)
                        (SRLl if oc % 2 == 0 else SRLlb).inc(sq.dma_start(
                            out=rlc[oc % 2][:],
                            in_=r1d[oc * 128:(oc + 1) * 128, hof:hof + 512]), 16)

                    @blk.vector
                    def _(v, oc=oc):
                        (SRLl if oc % 2 == 0 else SRLlb).wait(v)
                        if oc >= 2:
                            SR2E.wait(v, sr2_base + 2 * (oc - 2) + 1)
                        v.tensor_tensor(out=scr[:], in0=rlc[oc % 2][:], in1=aev[:],
                                        op=ALU.mult)
                        SX2.inc(v.tensor_tensor(out=x2c[:], in0=scr[:], in1=bev[:],
                                                op=ALU.subtract))

                    @blk.tensor
                    def _(te, oc=oc):
                        SAL.wait(te)
                        SW2[oc % 2].wait(te)
                        if oc >= 2:
                            SR2E.wait(te, sr2_base + 2 * (oc - 2) + 1)
                        for j in range(32):
                            te.matmul(py[oc % 2][:],
                                      ws2[oc % 2][:, j * 128:(j + 1) * 128],
                                      x2bf[:, j * 512:(j + 1) * 512],
                                      start=(j == 0), stop=False)
                        SMW.inc(te.matmul(py[oc % 2][:],
                                          ball_sb[:, oc * 128:(oc + 1) * 128],
                                          ztbf[:], start=False, stop=True))

                    @blk.vector
                    def _(v, oc=oc):
                        SMW.wait(v)
                        SX2.wait(v)
                        if oc >= 2:
                            SRL.wait(v)
                        SR2E.inc(v.scalar_tensor_tensor(
                            out=r1c[oc % 2][:], in0=py[oc % 2][:],
                            scalar=bbt[:, oc:oc + 1], in1=x2c[:],
                            op0=ALU.add, op1=ALU.add))

                    @blk.scalar
                    def _(sq, oc=oc):
                        SR2E.wait(sq)
                        SR2E.inc(sq.activation(out=sqs[oc % 2][:],
                                               in_=r1c[oc % 2][:], func=AF.Square))

                    @blk.tensor
                    def _(te, oc=oc):
                        SR2E.wait(te)
                        te.matmul(stA[:], cs["ONESF"][:], r1c[oc % 2][:],
                                  start=(oc == 0), stop=(oc == 31))
                        SST.inc(te.matmul(stB[:], cs["ONESF"][:], sqs[oc % 2][:],
                                          start=(oc == 0), stop=(oc == 31)))

                    @blk.sync
                    def _(q, oc=oc, hof=hof):
                        SST.wait(q)
                        SRL.inc(q.dma_start(
                            out=r2d[oc * 128:(oc + 1) * 128, hof:hof + 512],
                            in_=r1c[oc % 2][:]), 16)

                r2stores = SRL.v
                ln_rows("ln2")

                for oc in range(32):
                    @blk.scalar
                    def _(sq, oc=oc, hof=hof):
                        SST.wait(sq)
                        SRL.wait(sq, r2stores)
                        if oc >= 2:
                            SX2.wait(sq, SX2.v - 1)
                        else:
                            SX2.wait(sq, SX2.v)
                        (SRLl if oc % 2 == 0 else SRLlb).inc(sq.dma_start(
                            out=rlc[oc % 2][:],
                            in_=r2d[oc * 128:(oc + 1) * 128, hof:hof + 512]), 16)

                    @blk.vector
                    def _(v, oc=oc):
                        (SRLl if oc % 2 == 0 else SRLlb).wait(v)
                        if oc >= 1:
                            SYT.wait(v, SYT.v)
                        v.tensor_tensor(out=scr[:], in0=rlc[oc % 2][:], in1=aev[:],
                                        op=ALU.mult)
                        SX2.inc(v.tensor_tensor(out=x2c[:], in0=scr[:], in1=bev[:],
                                                op=ALU.subtract))

                    for tb in range(4):
                        k3 = SYT.v

                        @blk.tensor
                        def _(te, oc=oc, tb=tb, k3=k3):
                            SX2.wait(te)
                            if k3 >= 2:
                                SYE.wait(te, k3 - 1)
                            SYT.inc(te.transpose(ptr[k3 % 2][:, 0:128],
                                                 x2c[:, tb * 128:(tb + 1) * 128],
                                                 cs["IDF"][:]))

                        @blk.vector
                        def _(v, k3=k3):
                            SYT.wait(v)
                            if k3 >= 2:
                                SYD.wait(v, (k3 - 1) * 16)
                            SYE.inc(v.tensor_copy(out=ysb[k3 % 2][:],
                                                  in_=ptr[k3 % 2][:, 0:128]))

                        @blk.sync
                        def _(q, oc=oc, tb=tb, h2=h2, k3=k3):
                            SYE.wait(q)
                            SYD.inc(q.dma_start(
                                out=y_out[h2 * 512 + tb * 128:
                                          h2 * 512 + (tb + 1) * 128,
                                          oc * 128:(oc + 1) * 128],
                                in_=ysb[k3 % 2][:]), 16)

        @blk.sync
        def _(q):
            SYD.wait(q)

    return nc


def kernel(**inputs):
    x = np.ascontiguousarray(np.asarray(inputs["L0_attn_hidden_states"],
                                        dtype=np.float32).reshape(TOK, D))
    Wq = np.asarray(inputs["L0_attn_Wq"], dtype=np.float32)
    Wk = np.asarray(inputs["L0_attn_Wk"], dtype=np.float32)
    Wv = np.asarray(inputs["L0_attn_Wv"], dtype=np.float32)
    Wo = np.asarray(inputs["L0_attn_Wo"], dtype=np.float32)
    Wb = np.asarray(inputs["L0_ffn_W_base"], dtype=np.float32)
    b_base = np.asarray(inputs["L0_ffn_b_base"], dtype=np.float32)
    lora_A = np.asarray(inputs["L0_ffn_lora_A"], dtype=np.float32)
    lora_B = np.asarray(inputs["L0_ffn_lora_B"], dtype=np.float32)
    router_W = np.asarray(inputs["L0_ffn_router_W"], dtype=np.float32)
    router_b = np.asarray(inputs["L0_ffn_router_b"], dtype=np.float32)
    prior = np.asarray(inputs["L0_ffn_domain_prior"], dtype=np.float32)
    active = np.asarray(inputs["L0_ffn_active"], dtype=np.int32)

    consts = _host_consts()
    woT_r = np.ascontiguousarray(Wo.T.reshape(D, 32, 128).transpose(1, 0, 2))
    wbT_r = np.ascontiguousarray(Wb.T.reshape(D, 32, 128).transpose(1, 0, 2))
    aallT = np.ascontiguousarray(lora_A.reshape(ER, D).T)
    ball = np.ascontiguousarray(lora_B.transpose(0, 2, 1).reshape(ER, D))
    routerT = np.ascontiguousarray(router_W.T)
    b_base_t = np.ascontiguousarray(b_base.reshape(32, 128).T)

    in_maps = []
    for c in range(NCORES):
        m = {
            "x_slice": np.ascontiguousarray(x[c * TPC:(c + 1) * TPC, :]),
            "wqT": np.ascontiguousarray(Wq[c * 512:(c + 1) * 512, :].T),
            "wkT": np.ascontiguousarray(Wk[c * 128:(c + 1) * 128, :].T),
            "wvT": np.ascontiguousarray(Wv[c * 128:(c + 1) * 128, :].T),
            "woT": woT_r, "wbT": wbT_r, "aallT": aallT, "ball_d": ball,
            "routerT": routerT,
            "router_b": router_b.reshape(16, 1),
            "prior_in": prior.reshape(16, 1),
            "active_in": active.reshape(4, 1),
            "b_base_t": b_base_t,
        }
        for k in _CONST_SPECS:
            m[k] = consts[k]
        in_maps.append(m)

    nc = build_program()
    res = run_bass_kernel_spmd(nc, in_maps, core_ids=list(range(NCORES)))
    out = np.concatenate([np.asarray(res.results[c]["y_out"])
                          for c in range(NCORES)], axis=0)
    return out.reshape(B, S, D)
